# version 01 (final)
# baseline (speedup 1.0000x reference)
import numpy as np
import contextlib
import concourse.bass as bass
import concourse.mybir as mybir
from concourse.bass_utils import run_bass_kernel_spmd

F32 = mybir.dt.float32
BF16 = mybir.dt.bfloat16
AF = mybir.ActivationFunctionType
ALU = mybir.AluOpType
AX = mybir.AxisListType

D = 1024
TOK = 4096
NCORE = 8


def A_(v):
    if v is None or v is ... or isinstance(v, (int, float)):
        return v
    if type(v).__name__ == "AP":
        return v
    try:
        return v[:]
    except Exception:
        return v


class P:

    def __init__(self, nc):
        self.nc = nc
        self.ops = []
        self.cnt = 0

    def em(self, eng, fn, inc=1):
        self.ops.append((eng, fn, self.cnt, inc))
        self.cnt += inc

    def dma(self, out, in_):
        out, in_ = A_(out), A_(in_)
        self.em("sync", lambda: self.nc.sync.dma_start(out=out, in_=in_), 16)

    def mm(self, out, lhsT, rhs, start=True, stop=True):
        out, lhsT, rhs = A_(out), A_(lhsT), A_(rhs)
        self.em("tensor", lambda: self.nc.tensor.matmul(out, lhsT, rhs, start=start, stop=stop))

    def tr(self, out, in_, eye):
        out, in_, eye = A_(out), A_(in_), A_(eye)
        self.em("tensor", lambda: self.nc.tensor.transpose(out, in_, eye))

    def tt(self, out, a, b, op):
        out, a, b = A_(out), A_(a), A_(b)
        self.em("vector", lambda: self.nc.vector.tensor_tensor(out, a, b, op))

    def ts(self, out, a, s1, op0, s2=None, op1=None):
        out, a, s1 = A_(out), A_(a), A_(s1)
        if s2 is None:
            self.em("vector", lambda: self.nc.vector.tensor_scalar(out, a, s1, None, op0))
        else:
            self.em("vector", lambda: self.nc.vector.tensor_scalar(out, a, s1, s2, op0, op1))

    def stt(self, out, in0, sc, in1, op0, op1):
        out, in0, sc, in1 = A_(out), A_(in0), A_(sc), A_(in1)
        self.em("vector", lambda: self.nc.vector.scalar_tensor_tensor(out, in0, sc, in1, op0, op1))

    def red(self, out, in_, op, absv=None, axis=AX.X):
        out, in_ = A_(out), A_(in_)
        self.em("vector", lambda: self.nc.vector.tensor_reduce(out, in_, axis, op, apply_absolute_value=absv))

    def recip(self, out, in_):
        out, in_ = A_(out), A_(in_)
        self.em("vector", lambda: self.nc.vector.reciprocal(out, in_))

    def cp(self, out, in_):
        out, in_ = A_(out), A_(in_)
        self.em("vector", lambda: self.nc.vector.tensor_copy(out, in_))

    def act(self, out, in_, func, bias=None, scale=None):
        out, in_ = A_(out), A_(in_)
        kw = {}
        if bias is not None:
            kw["bias"] = A_(bias)
        if scale is not None:
            kw["scale"] = scale
        self.em("scalar", lambda: self.nc.scalar.activation(out, in_, func, **kw))

    def smul(self, out, in_, c):
        out, in_ = A_(out), A_(in_)
        self.em("scalar", lambda: self.nc.scalar.mul(out, in_, c))

    def memset(self, ap, v):
        ap = A_(ap)
        self.em("vector", lambda: self.nc.vector.memset(ap, v))

    def layernorm(self, xn, x, col, col2, sq, n=D):
        self.red(col, x, ALU.add)
        self.smul(col, col, 1.0 / n)
        self.ts(xn, x, col, ALU.subtract)
        self.act(sq, xn, AF.Square)
        self.red(col2, sq, ALU.add)
        self.smul(col2, col2, 1.0 / n)
        self.act(col2, col2, AF.Sqrt, bias=self.epscol)
        self.recip(col2, col2)
        self.ts(xn, xn, col2, ALU.mult)

    def finish(self, block):
        final = self.cnt
        nc = self.nc
        with nc.semaphore("gsem") as g:
            def mk(engn):
                def run(eng):
                    for (e, fn, wait, inc) in self.ops:
                        if e != engn:
                            continue
                        eng.wait_ge(g, wait)
                        fn().then_inc(g, inc)
                    eng.wait_ge(g, final)
                return run
            block.sync(mk("sync"))
            block.vector(mk("vector"))
            block.scalar(mk("scalar"))
            block.tensor(mk("tensor"))


def bcast_row(p, out_full, row_sb, ones_sb, psb, ncols):
    for c in range(0, ncols, 512):
        w = min(512, ncols - c)
        p.mm(psb[:, 0:w], ones_sb[0:1, :], row_sb[0:1, c:c + w])
        p.cp(out_full[:, c:c + w], psb[:, 0:w])


def build_l0ffn():
    nc = bass.Bass("TRN2", target_bir_lowering=False)
    xin = nc.dram_tensor("x", [TOK, D], F32, kind="ExternalInput").ap()
    wexp = nc.dram_tensor("wexp", [D, 4096], F32, kind="ExternalInput").ap()
    wcon = nc.dram_tensor("wcon", [4096, D], F32, kind="ExternalInput").ap()
    eye_d = nc.dram_tensor("eye", [128, 128], F32, kind="ExternalInput").ap()
    upA6 = nc.dram_tensor("upA6", [128, 7 * 32], F32, kind="ExternalInput").ap()
    upB6 = nc.dram_tensor("upB6", [128, 7 * 32], F32, kind="ExternalInput").ap()
    upBx6 = nc.dram_tensor("upBx6", [128, 7 * 32], F32, kind="ExternalInput").ap()
    dnA6 = nc.dram_tensor("dnA6", [128, 7 * 32], F32, kind="ExternalInput").ap()
    dnB6 = nc.dram_tensor("dnB6", [128, 7 * 32], F32, kind="ExternalInput").ap()
    dnBx6 = nc.dram_tensor("dnBx6", [128, 7 * 32], F32, kind="ExternalInput").ap()
    xmask = nc.dram_tensor("xmask", [128, 7 * 128], F32, kind="ExternalInput").ap()
    upA = nc.dram_tensor("upA", [128, 5 * 32], F32, kind="ExternalInput").ap()
    upB = nc.dram_tensor("upB", [128, 5 * 32], F32, kind="ExternalInput").ap()
    dnA = nc.dram_tensor("dnA", [128, 5 * 32], F32, kind="ExternalInput").ap()
    dnB = nc.dram_tensor("dnB", [128, 5 * 32], F32, kind="ExternalInput").ap()
    bexp = nc.dram_tensor("bexp", [128, 32], F32, kind="ExternalInput").ap()
    upbias = nc.dram_tensor("upbias", [128, 32], F32, kind="ExternalInput").ap()
    bcon = nc.dram_tensor("bcon", [128, 8], F32, kind="ExternalInput").ap()
    dbias = nc.dram_tensor("dbias", [128, 8], F32, kind="ExternalInput").ap()
    out = nc.dram_tensor("out", [TOK, D], F32, kind="ExternalOutput").ap()

    with contextlib.ExitStack() as _st:
        _E = _st.enter_context
        eyes = _E(nc.sbuf_tensor("eyes", [128, 128], F32))
        c6 = _E(nc.sbuf_tensor("c6", [128, 224 * 6], F32))
        xm = _E(nc.sbuf_tensor("xm", [128, 7 * 128], F32))
        c5 = _E(nc.sbuf_tensor("c5", [128, 160 * 4], F32))
        bia = _E(nc.sbuf_tensor("bia", [128, 88], F32))
        sa = _E(nc.sbuf_tensor("sa", [128, 128], F32))
        sb = _E(nc.sbuf_tensor("sb", [128, 128], F32))
        scm = _E(nc.sbuf_tensor("scm", [128, 128], F32))
        mb = _E(nc.sbuf_tensor("mb", [128, 128], F32))
        rup = _E(nc.sbuf_tensor("rup", [128, 32 * 128], BF16))
        rdn = _E(nc.sbuf_tensor("rdn", [128, 32 * 128], BF16))
        wstage = _E(nc.sbuf_tensor("wstage", [128, 1024], F32))
        wbuf = _E(nc.sbuf_tensor("wbuf", [128, 1024], BF16))
        xt = _E(nc.sbuf_tensor("xt", [128, 4 * D], F32))
        xn = _E(nc.sbuf_tensor("xn", [128, D], F32))
        xnT = _E(nc.sbuf_tensor("xnT", [128, 8 * 512], BF16))
        hbf = _E(nc.sbuf_tensor("hbf", [128, 512], BF16))
        u = _E(nc.sbuf_tensor("u", [128, 33 * 512], F32))
        zbf = _E(nc.sbuf_tensor("zbf", [128, 32 * 512], BF16))
        scr = _E(nc.sbuf_tensor("scr", [128, 512], F32))
        g1 = _E(nc.sbuf_tensor("g1", [128, 512], F32))
        g2 = _E(nc.sbuf_tensor("g2", [128, 512], F32))
        yac = _E(nc.sbuf_tensor("yac", [128, 8 * 512], F32))
        col = _E(nc.sbuf_tensor("col", [128, 1], F32))
        col2 = _E(nc.sbuf_tensor("col2", [128, 1], F32))
        sq = _E(nc.sbuf_tensor("sq", [128, D], F32))
        psA = _E(nc.psum_tensor("psA", [128, 512], F32))
        psB = _E(nc.psum_tensor("psB", [128, 512], F32))
        pst = _E(nc.psum_tensor("pst", [128, 128], F32))
        epscol = _E(nc.sbuf_tensor("epscol", [128, 1], F32))
        block = _E(nc.Block())
        p = P(nc)
        p.epscol = epscol
        p.memset(epscol, 1e-5)
        p.dma(eyes[:], eye_d[:])
        for i, t in enumerate([upA6, upB6, upBx6, dnA6, dnB6, dnBx6]):
            p.dma(c6[:, i * 224:(i + 1) * 224], t[:])
        p.dma(xm[:], xmask[:])
        for i, t in enumerate([upA, upB, dnA, dnB]):
            p.dma(c5[:, i * 160:(i + 1) * 160], t[:])
        p.dma(bia[:, 0:32], bexp[:])
        p.dma(bia[:, 32:64], upbias[:])
        p.dma(bia[:, 64:72], bcon[:])
        p.dma(bia[:, 72:80], dbias[:])
        p.tt(bia[:, 80:88], bia[:, 64:72], bia[:, 72:80], ALU.add)

        def a6(which, s, blk):
            base = {"upA": 0, "upB": 224, "upBx": 448, "dnA": 672, "dnB": 896, "dnBx": 1120}[which]
            return c6[:, base + s * 32 + blk: base + s * 32 + blk + 1]

        def a5(which, s, blk):
            base = {"upA": 0, "upB": 160, "dnA": 320, "dnB": 480}[which]
            return c5[:, base + s * 32 + blk: base + s * 32 + blk + 1]

        for pref, rbuf in (("up", rup), ("dn", rdn)):
            for blk in range(32):
                p.ts(sa, eyes[:], a6(pref + "A", 6, blk), ALU.mult)
                p.ts(sb, xm[:, 6 * 128:7 * 128], a6(pref + "Bx", 6, blk), ALU.mult)
                p.tt(mb, sa, sb, ALU.add)
                for s in range(5, -1, -1):
                    p.ts(sa, eyes[:], a6(pref + "A", s, blk), ALU.mult)
                    p.ts(sb, xm[:, s * 128:(s + 1) * 128], a6(pref + "B", s, blk), ALU.mult)
                    p.tt(scm, sa, sb, ALU.add)
                    p.mm(pst[:], scm, mb)
                    p.cp(mb, pst[:])
                p.cp(rbuf[:, blk * 128:(blk + 1) * 128], mb)

        for ch in range(8):
            for j in range(4):
                t0 = ch * 512 + j * 128
                p.dma(xt[:, j * D:(j + 1) * D], xin[t0:t0 + 128, :])
                p.layernorm(xn, xt[:, j * D:(j + 1) * D], col, col2, sq)
                for f in range(8):
                    p.tr(pst[:], xn[:, f * 128:(f + 1) * 128], eyes)
                    p.cp(xnT[:, f * 512 + j * 128: f * 512 + (j + 1) * 128], pst[:])
            for ct in range(8):
                p.dma(wstage[:], wexp[ct * 128:(ct + 1) * 128, 0:1024])
                p.cp(wbuf[:], wstage[:])
                for t in range(8):
                    p.mm(psA[:], wbuf[:, t * 128:(t + 1) * 128], xnT[:, ct * 512:(ct + 1) * 512])
                    if ct == 0:
                        p.cp(u[:, t * 512:(t + 1) * 512], psA[:])
                    else:
                        p.tt(u[:, t * 512:(t + 1) * 512], u[:, t * 512:(t + 1) * 512], psA[:], ALU.add)
                for seg in range(1, 4):
                    p.dma(wstage[:], wexp[ct * 128:(ct + 1) * 128, seg * 1024:(seg + 1) * 1024])
                    p.cp(wbuf[:], wstage[:])
                    for tt_ in range(8):
                        t = seg * 8 + tt_
                        p.mm(psA[:], wbuf[:, tt_ * 128:(tt_ + 1) * 128], xnT[:, ct * 512:(ct + 1) * 512])
                        if ct == 0:
                            p.cp(u[:, t * 512:(t + 1) * 512], psA[:])
                        else:
                            p.tt(u[:, t * 512:(t + 1) * 512], u[:, t * 512:(t + 1) * 512], psA[:], ALU.add)
            for t in range(32):
                p.ts(hbf[:], u[:, t * 512:(t + 1) * 512], bia[:, t:t + 1], ALU.add)
                p.mm(psB[:], rup[:, t * 128:(t + 1) * 128], hbf[:])
                p.cp(u[:, t * 512:(t + 1) * 512], psB[:])

            slot = list(range(32))
            spare = [32]

            def ut(s):
                return u[:, s * 512:(s + 1) * 512]

            def stages(pref, last_to_zbf):
                for si, s in enumerate(range(7, 12)):
                    stride = 2 ** (s - 7)
                    last = (s == 11)
                    for tA in range(32):
                        if (tA // stride) % 2 == 1:
                            continue
                        tB = tA + stride
                        A, B = slot[tA], slot[tB]
                        aA = a5(pref + "A", si, tA); bA = a5(pref + "B", si, tA)
                        aB = a5(pref + "A", si, tB); bB = a5(pref + "B", si, tB)
                        if last and last_to_zbf:
                            p.ts(scr, ut(B), bA, ALU.mult)
                            p.stt(zbf[:, tA * 512:(tA + 1) * 512], ut(A), aA, scr, ALU.mult, ALU.add)
                            p.ts(scr, ut(A), bB, ALU.mult)
                            p.stt(zbf[:, tB * 512:(tB + 1) * 512], ut(B), aB, scr, ALU.mult, ALU.add)
                        else:
                            p.ts(scr, ut(B), bA, ALU.mult)
                            p.stt(ut(spare[0]), ut(A), aA, scr, ALU.mult, ALU.add)
                            p.ts(scr, ut(A), bB, ALU.mult)
                            p.stt(ut(A), ut(B), aB, scr, ALU.mult, ALU.add)
                            slot[tA], slot[tB], spare[0] = spare[0], A, B

            stages("up", False)
            for t in range(32):
                A = ut(slot[t])
                p.ts(g1, A, bia[:, 32 + t:33 + t], ALU.add)
                p.act(g2, g1, AF.Erf, scale=0.7071067811865476)
                p.tt(g2, g1, g2, ALU.mult)
                p.tt(g2, g2, g1, ALU.add)
                p.smul(hbf[:], g2, 0.5)
                p.mm(psB[:], rdn[:, t * 128:(t + 1) * 128], hbf[:])
                p.cp(ut(slot[t]), psB[:])
            stages("dn", True)
            for f in range(32):
                p.dma(wstage[:], wcon[f * 128:(f + 1) * 128, :])
                p.cp(wbuf[:], wstage[:])
                for d in range(8):
                    p.mm(psA[:], wbuf[:, d * 128:(d + 1) * 128], zbf[:, f * 512:(f + 1) * 512])
                    if f == 0:
                        p.cp(yac[:, d * 512:(d + 1) * 512], psA[:])
                    else:
                        p.tt(yac[:, d * 512:(d + 1) * 512], yac[:, d * 512:(d + 1) * 512], psA[:], ALU.add)
            for d in range(8):
                p.ts(yac[:, d * 512:(d + 1) * 512], yac[:, d * 512:(d + 1) * 512],
                     bia[:, 80 + d:81 + d], ALU.add)
                for j in range(4):
                    p.tr(pst[:], yac[:, d * 512 + j * 128: d * 512 + (j + 1) * 128], eyes)
                    p.stt(xt[:, j * D + d * 128: j * D + (d + 1) * 128], pst[:], 0.5,
                          xt[:, j * D + d * 128: j * D + (d + 1) * 128], ALU.mult, ALU.add)
            for j in range(4):
                t0 = ch * 512 + j * 128
                p.dma(out[t0:t0 + 128, :], xt[:, j * D:(j + 1) * D])
        p.finish(block)
    return nc


def build_l0attn():
    nc = bass.Bass("TRN2", target_bir_lowering=False)
    xf = nc.dram_tensor("xf", [8192, D], F32, kind="ExternalInput").ap()
    xo = nc.dram_tensor("xo", [TOK, D], F32, kind="ExternalInput").ap()
    wqT = nc.dram_tensor("wqT", [D, D], F32, kind="ExternalInput").ap()
    wkT = nc.dram_tensor("wkT", [D, D], F32, kind="ExternalInput").ap()
    wvT = nc.dram_tensor("wvT", [D, D], F32, kind="ExternalInput").ap()
    bq = nc.dram_tensor("bq", [128, 8], F32, kind="ExternalInput").ap()
    bk = nc.dram_tensor("bk", [1, D], F32, kind="ExternalInput").ap()
    bv = nc.dram_tensor("bv", [1, D], F32, kind="ExternalInput").ap()
    gam = nc.dram_tensor("gam", [1, D], F32, kind="ExternalInput").ap()
    bet = nc.dram_tensor("bet", [1, D], F32, kind="ExternalInput").ap()
    ones = nc.dram_tensor("ones", [1, 128], F32, kind="ExternalInput").ap()
    eye_d = nc.dram_tensor("eye", [128, 128], F32, kind="ExternalInput").ap()
    out = nc.dram_tensor("out", [TOK, D], F32, kind="ExternalOutput").ap()

    with contextlib.ExitStack() as _st:
        _E = _st.enter_context
        eyes = _E(nc.sbuf_tensor("eyes", [128, 128], F32))
        onesb = _E(nc.sbuf_tensor("onesb", [1, 128], F32))
        onescol = _E(nc.sbuf_tensor("onescol", [128, 1], BF16))
        rows = _E(nc.sbuf_tensor("rows_s", [1, 4 * D], F32))
        full = _E(nc.sbuf_tensor("full", [128, 4 * D], F32))
        bqp = _E(nc.sbuf_tensor("bqp_s", [128, 8], F32))
        wq = _E(nc.sbuf_tensor("wq", [128, 8 * D], BF16))
        wk = _E(nc.sbuf_tensor("wk", [128, 8 * D], BF16))
        wv = _E(nc.sbuf_tensor("wv", [128, 8 * D], BF16))
        wst = _E(nc.sbuf_tensor("wst", [128, D], F32))
        xt = _E(nc.sbuf_tensor("xt", [128, 4 * D], F32))
        xn4 = _E(nc.sbuf_tensor("xn4", [128, 4 * D], F32))
        xnT = _E(nc.sbuf_tensor("xnT", [128, 8 * 512], BF16))
        kbf = _E(nc.sbuf_tensor("kbf", [128, D], BF16))
        vbf = _E(nc.sbuf_tensor("vbf", [128, D], BF16))
        scr = _E(nc.sbuf_tensor("scr", [128, 512], F32))
        kvb = _E(nc.sbuf_tensor("kvb", [128, 1024], BF16))
        kscol = _E(nc.sbuf_tensor("kscol", [128, 8], BF16))
        qT = _E(nc.sbuf_tensor("qT", [128, 8 * 512], BF16))
        den = _E(nc.sbuf_tensor("den", [128, 16], F32))
        yv = _E(nc.sbuf_tensor("yv", [128, D], F32))
        col = _E(nc.sbuf_tensor("col", [128, 1], F32))
        col2 = _E(nc.sbuf_tensor("col2", [128, 1], F32))
        sq = _E(nc.sbuf_tensor("sq", [128, D], F32))
        kv0 = _E(nc.psum_tensor("kv0", [64, 512], F32))
        kv1 = _E(nc.psum_tensor("kv1", [64, 512], F32))
        ksT = _E(nc.psum_tensor("ksT", [128, 8], F32))
        psA = _E(nc.psum_tensor("psA", [128, 512], F32))
        ctx0 = _E(nc.psum_tensor("ctx0", [128, 512], F32))
        ctx1 = _E(nc.psum_tensor("ctx1", [128, 512], F32))
        pst = _E(nc.psum_tensor("pst", [128, 128], F32))
        epscol = _E(nc.sbuf_tensor("epscol", [128, 1], F32))
        block = _E(nc.Block())
        p = P(nc)
        p.epscol = epscol
        p.memset(epscol, 1e-5)
        p.dma(eyes[:], eye_d[:])
        p.dma(onesb[:], ones[:])
        p.memset(onescol[:], 1.0)
        p.dma(bqp[:], bq[:])
        for i, t in enumerate([bk, bv, gam, bet]):
            p.dma(rows[:, i * D:(i + 1) * D], t[:])
        for i in range(4):
            bcast_row(p, full[:, i * D:(i + 1) * D], rows[:, i * D:(i + 1) * D], onesb, psA, D)
        for src, dst in ((wqT, wq), (wkT, wk), (wvT, wv)):
            for ct in range(8):
                p.dma(wst[:], src[ct * 128:(ct + 1) * 128, :])
                p.cp(dst[:, ct * D:(ct + 1) * D], wst[:])

        def ln_transp(j):
            p.layernorm(xn4[:, j * D:(j + 1) * D], xt[:, j * D:(j + 1) * D], col, col2, sq)
            for f in range(8):
                p.tr(pst[:], xn4[:, j * D + f * 128: j * D + (f + 1) * 128], eyes)
                p.cp(xnT[:, f * 512 + j * 128: f * 512 + (j + 1) * 128], pst[:])

        ntile = 0
        for ch in range(16):
            for j in range(4):
                t0 = ch * 512 + j * 128
                p.dma(xt[:, j * D:(j + 1) * D], xf[t0:t0 + 128, :])
                ln_transp(j)
            for j in range(4):
                for kind, wmat, dst in (("k", wk, kbf), ("v", wv, vbf)):
                    for ec in range(2):
                        for ct in range(8):
                            p.mm(psA[:], xnT[:, ct * 512 + j * 128: ct * 512 + (j + 1) * 128],
                                 wmat[:, ct * D + ec * 512: ct * D + (ec + 1) * 512],
                                 start=(ct == 0), stop=(ct == 7))
                        if kind == "k":
                            p.tt(scr[:], psA[:], full[:, ec * 512:(ec + 1) * 512], ALU.add)
                            p.act(dst[:, ec * 512:(ec + 1) * 512], scr[:], AF.Relu)
                        else:
                            p.tt(dst[:, ec * 512:(ec + 1) * 512], psA[:],
                                 full[:, D + ec * 512: D + (ec + 1) * 512], ALU.add)
                first = (ntile == 0); last = (ntile == 63)
                for h in range(16):
                    kvp = kv0 if h < 8 else kv1
                    p.mm(kvp[0:64, (h % 8) * 64:(h % 8 + 1) * 64],
                         kbf[:, h * 64:(h + 1) * 64], vbf[:, h * 64:(h + 1) * 64],
                         start=first, stop=last)
                for ec in range(8):
                    p.mm(ksT[:, ec:ec + 1], kbf[:, ec * 128:(ec + 1) * 128], onescol[:],
                         start=first, stop=last)
                ntile += 1
        p.cp(kvb[0:64, 0:512], kv0[0:64, :])
        p.cp(kvb[0:64, 512:1024], kv1[0:64, :])
        p.cp(kvb[64:128, 0:512], kv0[0:64, :])
        p.cp(kvb[64:128, 512:1024], kv1[0:64, :])
        p.cp(kscol[:], ksT[:])

        for ch in range(8):
            for j in range(4):
                t0 = ch * 512 + j * 128
                p.dma(xt[:, j * D:(j + 1) * D], xo[t0:t0 + 128, :])
                ln_transp(j)
            for t in range(8):
                for ct in range(8):
                    p.mm(psA[:], wq[:, ct * D + t * 128: ct * D + (t + 1) * 128],
                         xnT[:, ct * 512:(ct + 1) * 512], start=(ct == 0), stop=(ct == 7))
                p.act(qT[:, t * 512:(t + 1) * 512], psA[:], AF.Relu, bias=bqp[:, t:t + 1])
            for j in range(4):
                for h in range(16):
                    lhs = qT[(h % 2) * 64:(h % 2) * 64 + 64,
                             (h // 2) * 512 + j * 128: (h // 2) * 512 + (j + 1) * 128]
                    cxp = ctx0 if h < 8 else ctx1
                    p.mm(cxp[:, (h % 8) * 64:(h % 8 + 1) * 64], lhs,
                         kvb[(h % 2) * 64:(h % 2) * 64 + 64, h * 64:(h + 1) * 64])
                    p.mm(pst[:, h:h + 1], lhs,
                         kscol[(h % 2) * 64:(h % 2) * 64 + 64, h // 2:h // 2 + 1])
                p.ts(den[:], pst[:, 0:16], 1e-6, ALU.max)
                p.recip(den[:], den[:])
                p.tt(yv.rearrange("p (h e) -> p h e", e=64)[:, 0:8, :],
                     ctx0.rearrange("p (h e) -> p h e", e=64),
                     den[:, 0:8, None].broadcast_to([128, 8, 64]), ALU.mult)
                p.tt(yv.rearrange("p (h e) -> p h e", e=64)[:, 8:16, :],
                     ctx1.rearrange("p (h e) -> p h e", e=64),
                     den[:, 8:16, None].broadcast_to([128, 8, 64]), ALU.mult)
                p.tt(yv[:], yv[:], xn4[:, j * D:(j + 1) * D], ALU.add)
                zbuf = xn4[:, j * D:(j + 1) * D]
                p.layernorm(zbuf, yv[:], col, col2, sq)
                p.tt(zbuf, zbuf, full[:, 2 * D:3 * D], ALU.mult)
                p.tt(zbuf, zbuf, full[:, 3 * D:4 * D], ALU.add)
                p.tt(zbuf, zbuf, xt[:, j * D:(j + 1) * D], ALU.add)
                t0 = ch * 512 + j * 128
                p.dma(out[t0:t0 + 128, :], zbuf)
        p.finish(block)
    return nc


def fp4_quant(p, dst, src, nb, sc1, sc2, axv, qa, scrB):
    F = nb * 16
    p.red(sc1[:, 0:nb], src.rearrange("p (b i) -> p b i", i=16), ALU.max, absv=True)
    p.ts(sc1[:, 0:nb], sc1[:, 0:nb], 1.0 / 6.0, ALU.mult, 1e-12, ALU.max)
    p.act(axv[:, 0:F], src, AF.Abs)
    mids = [0.25, 0.75, 1.25, 1.75, 2.5, 3.5, 5.0]
    incr = [0.5, 0.5, 0.5, 0.5, 1.0, 1.0, 2.0]
    for i in range(7):
        p.ts(sc2[:, 0:nb], sc1[:, 0:nb], mids[i], ALU.mult)
        p.tt(scrB[:, 0:F].rearrange("p (b i) -> p b i", i=16),
             axv[:, 0:F].rearrange("p (b i) -> p b i", i=16),
             sc2[:, 0:nb, None].broadcast_to([128, nb, 16]), ALU.is_gt)
        if i == 0:
            p.ts(qa[:, 0:F], scrB[:, 0:F], 0.5, ALU.mult)
        else:
            p.stt(qa[:, 0:F], scrB[:, 0:F], incr[i], qa[:, 0:F], ALU.mult, ALU.add)
    p.act(axv[:, 0:F], src, AF.Sign)
    p.tt(qa[:, 0:F], qa[:, 0:F], axv[:, 0:F], ALU.mult)
    p.tt(dst.rearrange("p (b i) -> p b i", i=16), qa[:, 0:F].rearrange("p (b i) -> p b i", i=16),
         sc1[:, 0:nb, None].broadcast_to([128, nb, 16]), ALU.mult)


def build_l1ffn():
    nc = bass.Bass("TRN2", target_bir_lowering=False)
    II = 2816
    xin = nc.dram_tensor("x", [TOK, D], F32, kind="ExternalInput").ap()
    wg = nc.dram_tensor("wg", [II, D], F32, kind="ExternalInput").ap()
    wu = nc.dram_tensor("wu", [II, D], F32, kind="ExternalInput").ap()
    wd = nc.dram_tensor("wd", [D, II], F32, kind="ExternalInput").ap()
    eye_d = nc.dram_tensor("eye", [128, 128], F32, kind="ExternalInput").ap()
    out = nc.dram_tensor("out", [TOK, D], F32, kind="ExternalOutput").ap()
    wdTs = nc.dram_tensor("wdTs", [128, 22 * D], BF16, kind="Internal").ap()
    with contextlib.ExitStack() as _st:
        _E = _st.enter_context
        eyes = _E(nc.sbuf_tensor("eyes", [128, 128], F32))
        wstage = _E(nc.sbuf_tensor("wstage", [128, II], F32))
        wgT = _E(nc.sbuf_tensor("wgT", [128, 8 * II], BF16))
        wuT = _E(nc.sbuf_tensor("wuT", [128, 8 * II], BF16))
        wdT = _E(nc.sbuf_tensor("wdT", [128, D], BF16))
        wdbuf = _E(nc.sbuf_tensor("wdbuf", [128, 512], BF16))
        xt = _E(nc.sbuf_tensor("xt", [128, D], F32))
        xn = _E(nc.sbuf_tensor("xn", [128, D], F32))
        xqT = _E(nc.sbuf_tensor("xqT", [128, 8 * 128], BF16))
        gg = _E(nc.sbuf_tensor("gg", [128, II], F32))
        uu = _E(nc.sbuf_tensor("uu", [128, II], F32))
        hqT = _E(nc.sbuf_tensor("hqT", [128, 22 * 128], BF16))
        sc1 = _E(nc.sbuf_tensor("sc1", [128, 176], F32))
        sc2 = _E(nc.sbuf_tensor("sc2", [128, 176], F32))
        axv = _E(nc.sbuf_tensor("axv", [128, II], F32))
        qa = _E(nc.sbuf_tensor("qa", [128, II], F32))
        col = _E(nc.sbuf_tensor("col", [128, 1], F32))
        col2 = _E(nc.sbuf_tensor("col2", [128, 1], F32))
        sq = _E(nc.sbuf_tensor("sq", [128, D], F32))
        psA = _E(nc.psum_tensor("psA", [128, 512], F32))
        pst = _E(nc.psum_tensor("pst", [128, 128], F32))
        epscol = _E(nc.sbuf_tensor("epscol", [128, 1], F32))
        block = _E(nc.Block())
        p = P(nc)
        p.epscol = epscol
        p.memset(epscol, 1e-5)
        scrB = gg
        p.dma(eyes[:], eye_d[:])
        for src, dstT, ntile in ((wg, wgT, 22), (wu, wuT, 22)):
            for r in range(ntile):
                p.dma(wstage[:, 0:D], src[r * 128:(r + 1) * 128, :])
                fp4_quant(p, wstage[:, 0:D], wstage[:, 0:D], 64, sc1, sc2, axv, qa, scrB)
                for c in range(8):
                    p.tr(pst[:], wstage[:, c * 128:(c + 1) * 128], eyes)
                    p.cp(dstT[:, c * II + r * 128: c * II + (r + 1) * 128], pst[:])
        for r in range(8):
            p.dma(wstage[:], wd[r * 128:(r + 1) * 128, :])
            fp4_quant(p, wstage[:], wstage[:], 176, sc1, sc2, axv, qa, scrB)
            for c in range(22):
                p.tr(pst[:], wstage[:, c * 128:(c + 1) * 128], eyes)
                p.cp(wdT[:, r * 128:(r + 1) * 128], pst[:])
                p.dma(wdTs[:, c * D + r * 128: c * D + (r + 1) * 128], wdT[:, r * 128:(r + 1) * 128])
        for ch in range(8):
            for j in range(4):
                t0 = ch * 512 + j * 128
                p.dma(xt[:], xin[t0:t0 + 128, :])
                p.layernorm(xn, xt[:], col, col2, sq)
                fp4_quant(p, xn[:], xn[:], 64, sc1, sc2, axv, qa, scrB)
                for c in range(8):
                    p.tr(pst[:], xn[:, c * 128:(c + 1) * 128], eyes)
                    p.cp(xqT[:, c * 128:(c + 1) * 128], pst[:])
                for kind, wT, dst in (("g", wgT, gg), ("u", wuT, uu)):
                    for ic in range(6):
                        w = min(512, II - ic * 512)
                        for ct in range(8):
                            p.mm(psA[:, 0:w], xqT[:, ct * 128:(ct + 1) * 128],
                                 wT[:, ct * II + ic * 512: ct * II + ic * 512 + w],
                                 start=(ct == 0), stop=(ct == 7))
                        if kind == "g":
                            p.act(dst[:, ic * 512: ic * 512 + w], psA[:, 0:w], AF.Silu)
                        else:
                            p.cp(dst[:, ic * 512: ic * 512 + w], psA[:, 0:w])
                p.tt(uu[:], gg[:], uu[:], ALU.mult)
                fp4_quant(p, uu[:], uu[:], 176, sc1, sc2, axv, qa, scrB)
                for c in range(22):
                    p.tr(pst[:], uu[:, c * 128:(c + 1) * 128], eyes)
                    p.cp(hqT[:, c * 128:(c + 1) * 128], pst[:])
                for dc in range(2):
                    for kt in range(22):
                        p.dma(wdbuf[:], wdTs[:, kt * D + dc * 512: kt * D + (dc + 1) * 512])
                        p.mm(psA[:], hqT[:, kt * 128:(kt + 1) * 128], wdbuf[:],
                             start=(kt == 0), stop=(kt == 21))
                    p.stt(xt[:, dc * 512:(dc + 1) * 512], psA[:], 0.5,
                          xt[:, dc * 512:(dc + 1) * 512], ALU.mult, ALU.add)
                p.dma(out[t0:t0 + 128, :], xt[:])
        p.finish(block)
    return nc


def build_l1attn():
    nc = bass.Bass("TRN2", target_bir_lowering=False)
    xe = nc.dram_tensor("xe", [4608, D], F32, kind="ExternalInput").ap()
    wqd = nc.dram_tensor("wq", [D, D], F32, kind="ExternalInput").ap()
    wkd = nc.dram_tensor("wk", [D, D], F32, kind="ExternalInput").ap()
    wvd = nc.dram_tensor("wv", [D, D], F32, kind="ExternalInput").ap()
    wod = nc.dram_tensor("wo", [D, D], F32, kind="ExternalInput").ap()
    w1d = nc.dram_tensor("w1", [D, 4096], F32, kind="ExternalInput").ap()
    w2d = nc.dram_tensor("w2", [4096, D], F32, kind="ExternalInput").ap()
    bqp_d = nc.dram_tensor("bqp", [128, 8], F32, kind="ExternalInput").ap()
    bkp_d = nc.dram_tensor("bkp", [128, 8], F32, kind="ExternalInput").ap()
    c1p_d = nc.dram_tensor("c1p", [128, 32], F32, kind="ExternalInput").ap()
    rows_d = nc.dram_tensor("rows", [1, 7 * D], F32, kind="ExternalInput").ap()
    ones_d = nc.dram_tensor("ones", [1, 128], F32, kind="ExternalInput").ap()
    eye_d = nc.dram_tensor("eye", [128, 128], F32, kind="ExternalInput").ap()
    mmid_d = nc.dram_tensor("mmid", [128, 640], F32, kind="ExternalInput").ap()
    mspec_d = nc.dram_tensor("mspec", [4, 128, 640], F32, kind="ExternalInput").ap()
    out = nc.dram_tensor("out", [TOK, D], F32, kind="ExternalOutput").ap()
    with contextlib.ExitStack() as _st:
        _E = _st.enter_context
        eyes = _E(nc.sbuf_tensor("eyes", [128, 128], F32))
        eyeb = _E(nc.sbuf_tensor("eyeb", [128, 128], BF16))
        onesb = _E(nc.sbuf_tensor("onesb", [1, 128], F32))
        bqp = _E(nc.sbuf_tensor("bqp_s", [128, 8], F32))
        bkp = _E(nc.sbuf_tensor("bkp_s", [128, 8], F32))
        c1p = _E(nc.sbuf_tensor("c1p_s", [128, 32], F32))
        rows = _E(nc.sbuf_tensor("rows_s", [1, D], F32))
        full = _E(nc.sbuf_tensor("full", [128, 7 * D], F32))
        msk = _E(nc.sbuf_tensor("msk", [128, 5 * 640], F32))
        wb = _E(nc.sbuf_tensor("wb", [128, 8 * D], BF16))
        w1b = _E(nc.sbuf_tensor("w1b", [128, 1024], BF16))
        w2b = _E(nc.sbuf_tensor("w2b", [128, D], BF16))
        wst = _E(nc.sbuf_tensor("wst", [128, D], F32))
        xno = _E(nc.sbuf_tensor("xno", [128, D], F32))
        xnT = _E(nc.sbuf_tensor("xnT", [128, 8 * 1024], BF16))
        qT = _E(nc.sbuf_tensor("qT", [128, 8 * 512], BF16))
        kT = _E(nc.sbuf_tensor("kT", [128, 8 * 1024], BF16))
        vb = _E(nc.sbuf_tensor("vb", [128, 8 * D], BF16))
        ssb = _E(nc.sbuf_tensor("ssb", [128, 4 * 640], F32))
        smx = _E(nc.sbuf_tensor("smx", [128, 4], F32))
        abf = _E(nc.sbuf_tensor("abf", [128, 4 * 640], BF16))
        aT = _E(nc.sbuf_tensor("aT", [128, 640], BF16))
        ctxb = _E(nc.sbuf_tensor("ctxb", [128, 8 * 128], BF16))
        h1 = _E(nc.sbuf_tensor("h1", [128, D], F32))
        hk = _E(nc.sbuf_tensor("hk", [128, 4 * D], F32))
        hT = _E(nc.sbuf_tensor("hT", [128, 8 * 512], BF16))
        gbf = _E(nc.sbuf_tensor("gbf", [128, 512], BF16))
        f2a = _E(nc.sbuf_tensor("f2a", [128, 4 * D], F32))
        s1 = _E(nc.sbuf_tensor("s1", [128, 512], F32))
        s2 = _E(nc.sbuf_tensor("s2", [128, 512], F32))
        s3 = _E(nc.sbuf_tensor("s3", [128, 512], F32))
        s4 = _E(nc.sbuf_tensor("s4", [128, 512], F32))
        scr1k = _E(nc.sbuf_tensor("scr1k", [128, D], F32))
        col = _E(nc.sbuf_tensor("col", [128, 1], F32))
        col2 = _E(nc.sbuf_tensor("col2", [128, 1], F32))
        psA = _E(nc.psum_tensor("psA", [128, 512], F32))
        psb2 = _E(nc.psum_tensor("psb2", [128, 128], F32))
        sc0 = _E(nc.psum_tensor("sc0", [128, 512], F32))
        sc1p = _E(nc.psum_tensor("sc1p", [128, 128], F32))
        ptb = _E(nc.psum_tensor("ptb", [128, 128], BF16))
        cx0 = _E(nc.psum_tensor("cx0", [128, 512], F32))
        epscol = _E(nc.sbuf_tensor("epscol", [128, 1], F32))
        block = _E(nc.Block())
        p = P(nc)
        p.epscol = epscol
        p.memset(epscol, 1e-5)
        p.dma(eyes[:], eye_d[:])
        p.cp(eyeb[:], eyes[:])
        p.dma(onesb[:], ones_d[:])
        p.dma(bqp[:], bqp_d[:])
        p.dma(bkp[:], bkp_d[:])
        p.dma(c1p[:], c1p_d[:])
        for i in range(7):
            p.dma(rows[:], rows_d[:, i * D:(i + 1) * D])
            bcast_row(p, full[:, i * D:(i + 1) * D], rows[:], onesb, psA, D)
        p.dma(msk[:, 0:640], mmid_d[:])
        for i in range(4):
            p.dma(msk[:, (i + 1) * 640:(i + 2) * 640], mspec_d[i, :, :])
        def load_w(src):
            for ct in range(8):
                p.dma(wst[:], src[ct * 128:(ct + 1) * 128, :])
                p.cp(wb[:, ct * D:(ct + 1) * D], wst[:])

        for ch in range(8):
            for j in range(8):
                t0 = ch * 512 + j * 128
                p.dma(scr1k[:], xe[t0:t0 + 128, :])
                p.layernorm(scr1k, scr1k[:], col, col2, f2a[:, 0:D])
                for f in range(8):
                    p.tr(psb2[:], scr1k[:, f * 128:(f + 1) * 128], eyes)
                    p.cp(xnT[:, f * 1024 + j * 128: f * 1024 + (j + 1) * 128], psb2[:])
            load_w(wqd)
            for t in range(8):
                for ct in range(8):
                    p.mm(psA[:], wb[:, ct * D + t * 128: ct * D + (t + 1) * 128],
                         xnT[:, ct * 1024 + 256: ct * 1024 + 768], start=(ct == 0), stop=(ct == 7))
                p.ts(qT[:, t * 512:(t + 1) * 512], psA[:], bqp[:, t:t + 1], ALU.add)
            load_w(wkd)
            for t in range(8):
                for ec in range(2):
                    for ct in range(8):
                        p.mm(psA[:], wb[:, ct * D + t * 128: ct * D + (t + 1) * 128],
                             xnT[:, ct * 1024 + ec * 512: ct * 1024 + (ec + 1) * 512],
                             start=(ct == 0), stop=(ct == 7))
                    p.ts(kT[:, t * 1024 + ec * 512: t * 1024 + (ec + 1) * 512], psA[:],
                         bkp[:, t:t + 1], ALU.add)
            load_w(wvd)
            for j in range(8):
                for ec in range(2):
                    for ct in range(8):
                        p.mm(psA[:], xnT[:, ct * 1024 + j * 128: ct * 1024 + (j + 1) * 128],
                             wb[:, ct * D + ec * 512: ct * D + (ec + 1) * 512],
                             start=(ct == 0), stop=(ct == 7))
                    p.tt(vb[:, j * D + ec * 512: j * D + (ec + 1) * 512], psA[:],
                         full[:, ec * 512:(ec + 1) * 512], ALU.add)
            load_w(wod)
            for jt in range(4):
                gt = ch * 4 + jt
                if gt == 0: mk = msk[:, 640:1280]
                elif gt == 1: mk = msk[:, 1280:1920]
                elif gt == 30: mk = msk[:, 1920:2560]
                elif gt == 31: mk = msk[:, 2560:3200]
                else: mk = msk[:, 0:640]
                for grp in range(4):
                    for hh in range(4):
                        h = grp * 4 + hh
                        lhs = qT[(h % 2) * 64:(h % 2) * 64 + 64,
                                 (h // 2) * 512 + jt * 128:(h // 2) * 512 + (jt + 1) * 128]
                        kbase = (h // 2) * 1024 + jt * 128
                        koff = (h % 2) * 64
                        p.mm(sc0[:], lhs, kT[koff:koff + 64, kbase:kbase + 512])
                        p.mm(sc1p[:], lhs, kT[koff:koff + 64, kbase + 512:kbase + 640])
                        p.stt(ssb[:, hh * 640:hh * 640 + 512], sc0[:], 0.125, mk[:, 0:512],
                              ALU.mult, ALU.add)
                        p.stt(ssb[:, hh * 640 + 512:(hh + 1) * 640], sc1p[:], 0.125,
                              mk[:, 512:640], ALU.mult, ALU.add)
                    p.red(smx[:], ssb.rearrange("p (h w) -> p h w", w=640), ALU.max)
                    p.tt(ssb.rearrange("p (h w) -> p h w", w=640),
                         ssb.rearrange("p (h w) -> p h w", w=640),
                         smx[:, :, None].broadcast_to([128, 4, 640]), ALU.subtract)
                    p.act(ssb[:], ssb[:], AF.Exp)
                    p.red(smx[:], ssb.rearrange("p (h w) -> p h w", w=640), ALU.add)
                    p.recip(smx[:], smx[:])
                    p.tt(abf.rearrange("p (h w) -> p h w", w=640),
                         ssb.rearrange("p (h w) -> p h w", w=640),
                         smx[:, :, None].broadcast_to([128, 4, 640]), ALU.mult)
                    for hh in range(4):
                        h = grp * 4 + hh
                        for r in range(5):
                            p.tr(ptb[:], abf[:, hh * 640 + r * 128: hh * 640 + (r + 1) * 128], eyeb)
                            p.cp(aT[:, r * 128:(r + 1) * 128], ptb[:])
                        et = h // 2
                        for r in range(5):
                            p.mm(cx0[(h % 2) * 64:(h % 2) * 64 + 64, (et % 4) * 128:(et % 4 + 1) * 128],
                                 vb[:, (jt + r) * D + h * 64: (jt + r) * D + (h + 1) * 64],
                                 aT[:, r * 128:(r + 1) * 128], start=(r == 0), stop=(r == 4))
                    if grp % 2 == 1:
                        p.cp(ctxb[:, (grp // 2) * 512:(grp // 2 + 1) * 512], cx0[:])
                t0x = ch * 512 + (jt + 2) * 128
                p.dma(f2a[:, 0:D], xe[t0x:t0x + 128, :])
                p.layernorm(xno, f2a[:, 0:D], col, col2, hk[:, jt * D:(jt + 1) * D])
                for ec in range(2):
                    for et in range(8):
                        p.mm(psA[:], ctxb[:, et * 128:(et + 1) * 128],
                             wb[:, et * D + ec * 512: et * D + (ec + 1) * 512],
                             start=(et == 0), stop=(et == 7))
                    p.tt(s1[:], psA[:], full[:, D + ec * 512: D + (ec + 1) * 512], ALU.add)
                    p.tt(h1[:, ec * 512:(ec + 1) * 512], s1[:],
                         xno[:, ec * 512:(ec + 1) * 512], ALU.add)
                hkj = hk[:, jt * D:(jt + 1) * D]
                p.layernorm(hkj, h1[:], col, col2, scr1k)
                p.tt(hkj, hkj, full[:, 3 * D:4 * D], ALU.mult)
                p.tt(hkj, hkj, full[:, 4 * D:5 * D], ALU.add)
                for f in range(8):
                    p.tr(psb2[:], hkj[:, f * 128:(f + 1) * 128], eyes)
                    p.cp(hT[:, f * 512 + jt * 128: f * 512 + (jt + 1) * 128], psb2[:])
            for f in range(32):
                for ct in range(8):
                    p.dma(wst[:, 0:128], w1d[ct * 128:(ct + 1) * 128, f * 128:(f + 1) * 128])
                    p.cp(w1b[:, ct * 128:(ct + 1) * 128], wst[:, 0:128])
                for ct in range(8):
                    p.mm(psA[:], w1b[:, ct * 128:(ct + 1) * 128],
                         hT[:, ct * 512:(ct + 1) * 512], start=(ct == 0), stop=(ct == 7))
                p.ts(s1[:], psA[:], c1p[:, f:f + 1], ALU.add)
                p.act(s2[:], s1[:], AF.Square)
                p.tt(s3[:], s2[:], s1[:], ALU.mult)
                p.stt(s3[:], s3[:], 0.044715, s1[:], ALU.mult, ALU.add)
                p.act(s2[:], s3[:], AF.Tanh, scale=0.7978845608028654)
                p.tt(s4[:], s1[:], s2[:], ALU.mult)
                p.tt(s4[:], s4[:], s1[:], ALU.add)
                p.smul(gbf[:], s4[:], 0.5)
                p.dma(wst[:], w2d[f * 128:(f + 1) * 128, :])
                p.cp(w2b[:], wst[:])
                for j in range(4):
                    for ec in range(2):
                        p.mm(psA[:], gbf[:, j * 128:(j + 1) * 128], w2b[:, ec * 512:(ec + 1) * 512])
                        if f == 0:
                            p.cp(f2a[:, j * D + ec * 512: j * D + (ec + 1) * 512], psA[:])
                        else:
                            p.tt(f2a[:, j * D + ec * 512: j * D + (ec + 1) * 512],
                                 f2a[:, j * D + ec * 512: j * D + (ec + 1) * 512], psA[:], ALU.add)
            for j in range(4):
                fj = f2a[:, j * D:(j + 1) * D]
                p.tt(fj, fj, full[:, 2 * D:3 * D], ALU.add)
                p.tt(fj, fj, hk[:, j * D:(j + 1) * D], ALU.add)
                p.layernorm(h1, fj, col, col2, scr1k)
                p.tt(h1, h1, full[:, 5 * D:6 * D], ALU.mult)
                p.tt(h1, h1, full[:, 6 * D:7 * D], ALU.add)
                t0 = ch * 512 + (j + 2) * 128
                p.dma(scr1k[:], xe[t0:t0 + 128, :])
                p.tt(h1, h1, scr1k[:], ALU.add)
                p.dma(out[ch * 512 + j * 128: ch * 512 + j * 128 + 128, :], h1)
        p.finish(block)
    return nc


_CACHE = {}


def _alphabeta(weights, stage, dim=4096):
    s2 = 2 ** stage
    nb = dim // (2 * s2)
    w = np.asarray(weights[stage]).reshape(nb, s2, 2, 2)
    f = np.arange(dim)
    blk = f // (2 * s2)
    half = (f // s2) % 2
    j = f % s2
    alpha = w[blk, j, half, half].astype(np.float32)
    beta = w[blk, j, half, 1 - half].astype(np.float32)
    return alpha, beta


def _pt(v, nt):
    return np.ascontiguousarray(np.asarray(v, np.float32).reshape(nt, 128).T)


def kernel(**inp):
    for k in list(inp):
        inp[k] = np.asarray(inp[k], np.float32)
    x = inp["L0_attn_x"].reshape(4, 8192, D)
    eye = np.eye(128, dtype=np.float32)
    ones = np.ones((1, 128), np.float32)

    if "progs" not in _CACHE:
        _CACHE["progs"] = {
            "A": build_l0ffn(), "B": build_l0attn(),
            "C": build_l1ffn(), "D": build_l1attn(),
        }
    progs = _CACHE["progs"]

    upw, dnw = inp["L0_ffn_up_weights"], inp["L0_ffn_down_weights"]

    def fold_consts(wts):
        A6 = np.concatenate([_pt(_alphabeta(wts, s)[0], 32) for s in range(7)], 1)
        B6 = np.concatenate([_pt(_alphabeta(wts, s)[1], 32) for s in range(7)], 1)
        Bx6 = []
        for s in range(7):
            a, b = _alphabeta(wts, s)
            Bx6.append(_pt(b[np.arange(4096) ^ (2 ** s)], 32))
        Bx6 = np.concatenate(Bx6, 1)
        A5 = np.concatenate([_pt(_alphabeta(wts, s)[0], 32) for s in range(7, 12)], 1)
        B5 = np.concatenate([_pt(_alphabeta(wts, s)[1], 32) for s in range(7, 12)], 1)
        return A6, B6, Bx6, A5, B5

    uA6, uB6, uBx6, uA5, uB5 = fold_consts(upw)
    dA6, dB6, dBx6, dA5, dB5 = fold_consts(dnw)
    xmask = np.zeros((128, 7 * 128), np.float32)
    for s in range(7):
        q = np.arange(128)
        xmask[q, s * 128 + (q ^ (2 ** s))] = 1.0
    inA_static = dict(
        wexp=inp["L0_ffn_W_exp"], wcon=inp["L0_ffn_W_con"], eye=eye,
        upA6=uA6, upB6=uB6, upBx6=uBx6, dnA6=dA6, dnB6=dB6, dnBx6=dBx6,
        xmask=xmask, upA=uA5, upB=uB5, dnA=dA5, dnB=dB5,
        bexp=_pt(inp["L0_ffn_b_exp"], 32), upbias=_pt(inp["L0_ffn_up_bias"], 32),
        bcon=_pt(inp["L0_ffn_b_con"], 8), dbias=_pt(inp["L0_ffn_down_bias"], 8),
    )
    inB_static = dict(
        wqT=np.ascontiguousarray(inp["L0_attn_Wq"].T),
        wkT=np.ascontiguousarray(inp["L0_attn_Wk"].T),
        wvT=np.ascontiguousarray(inp["L0_attn_Wv"].T),
        bq=_pt(inp["L0_attn_bq"], 8), bk=inp["L0_attn_bk"][None, :], bv=inp["L0_attn_bv"][None, :],
        gam=inp["L0_attn_gamma"][None, :], bet=inp["L0_attn_beta"][None, :],
        ones=ones, eye=eye,
    )
    inC_static = dict(
        wg=inp["L1_ffn_w_gate"], wu=inp["L1_ffn_w_up"], wd=inp["L1_ffn_w_down"], eye=eye,
    )
    rows7 = np.ascontiguousarray(np.concatenate(
        [inp["L1_attn_bv"], inp["L1_attn_bo"], inp["L1_attn_c2"],
         inp["L1_attn_g1"], inp["L1_attn_b1"],
         inp["L1_attn_g2"], inp["L1_attn_b2"]])[None, :])
    i_ = np.arange(128)[:, None]
    j_ = np.arange(640)[None, :]

    def band_mask(gt, hf):
        m = np.where(np.abs(i_ + 256 - j_) <= 256, 0.0, -1e9).astype(np.float32)
        kpos = hf * 4096 + gt * 128 + j_ - 256
        m = np.where((kpos >= 0) & (kpos < 8192), m, np.float32(-1e9))
        return np.ascontiguousarray(np.broadcast_to(m, (128, 640)).astype(np.float32))

    inD_static_core = []
    for c in range(NCORE):
        hf = c % 2
        spec = np.ascontiguousarray(np.stack(
            [band_mask(0, hf), band_mask(1, hf), band_mask(30, hf), band_mask(31, hf)]))
        inD_static_core.append(dict(
            wq=inp["L1_attn_Wq"], wk=inp["L1_attn_Wk"], wv=inp["L1_attn_Wv"], wo=inp["L1_attn_Wo"],
            w1=inp["L1_attn_W1"], w2=inp["L1_attn_W2"],
            bqp=_pt(inp["L1_attn_bq"], 8), bkp=_pt(inp["L1_attn_bk"], 8),
            c1p=_pt(inp["L1_attn_c1"], 32), rows=rows7, ones=ones, eye=eye,
            mmid=band_mask(15, 0), mspec=spec,
        ))

    def shards(xc):
        return [np.ascontiguousarray(xc[c // 2, (c % 2) * TOK:(c % 2 + 1) * TOK, :])
                for c in range(NCORE)]

    def run(tag, maps):
        res = run_bass_kernel_spmd(progs[tag], maps, core_ids=list(range(NCORE)))
        outs = [np.asarray(res.results[c]["out"], np.float32) for c in range(NCORE)]
        return np.stack([np.concatenate([outs[2 * b], outs[2 * b + 1]], 0)
                         for b in range(4)])

    xc = x
    for step in ["A", "B", "A", "C", "D", "C"]:
        sh = shards(xc)
        if step == "A":
            maps = [dict(inA_static, x=sh[c]) for c in range(NCORE)]
        elif step == "B":
            maps = [dict(inB_static, xf=np.ascontiguousarray(xc[c // 2]), xo=sh[c])
                    for c in range(NCORE)]
        elif step == "C":
            maps = [dict(inC_static, x=sh[c]) for c in range(NCORE)]
        else:
            maps = []
            for c in range(NCORE):
                hf = c % 2
                b = c // 2
                if hf == 0:
                    ext = np.concatenate([np.zeros((256, D), np.float32), xc[b, 0:4352]], 0)
                else:
                    ext = np.concatenate([xc[b, 3840:8192], np.zeros((256, D), np.float32)], 0)
                maps.append(dict(inD_static_core[c], xe=np.ascontiguousarray(ext)))
        xc = run(step, maps)
    return xc.astype(np.float32)
